# version 01 (final)
# baseline (speedup 1.0000x reference)
from contextlib import ExitStack
import numpy as np
import concourse.bass as bass
import concourse.mybir as mybir
from concourse.bass_utils import run_bass_kernel_spmd

F32 = mybir.dt.float32
F16 = mybir.dt.float16
AF = mybir.ActivationFunctionType
ALU = mybir.AluOpType
AX = mybir.AxisListType

NCORES = 8
T = 1024
TP = 512
H = 2048
HC = H // 128
D = 128
I0 = 7168
IC0 = I0 // 128
E8 = 8
GRP = 2
NH = 16
HD = 128
N1 = 8192
SCALING = 16.0 / 64.0
EPS0 = 1.0e-6
LNEPS = 1.0e-5
BIG = 1.0e30
SYNC_HEAVY = False
CC_DELAY = 0

ENGINES = ("tensor", "vector", "scalar", "gpsimd", "sync")


class Sem:
    def __init__(self, handle, name):
        self.h = handle
        self.name = name
        self.n = 0


class Prog:

    def __init__(self, nc):
        self.nc = nc
        self.streams = {e: [] for e in ENGINES}
        self._waited = {}
        self.self_sems = {}

    def sem(self, name):
        return Sem(self.nc.alloc_semaphore(name=name), name)

    def on(self, eng, fn):
        self.streams[eng].append(fn)

    def wait(self, eng, sem, count=None):
        if count is None:
            count = sem.n
        assert count <= sem.n, f"wait on future count of {sem.name}: {count}>{sem.n}"
        if count <= 0:
            return
        key = (eng, sem.name)
        if self._waited.get(key, 0) >= count:
            return
        self._waited[key] = count
        self.on(eng, lambda e, h=sem.h, c=count: e.wait_ge(h, c))

    def run(self, eng, fn, inc=None, amt=1, serial=True):
        ss = self.self_sems.get(eng)
        if serial and ss is not None:
            self.wait(eng, ss, ss.n)
            inc = ss
            amt = 1
        if inc is None:
            self.on(eng, fn)
            return None
        inc.n += amt
        self.on(eng, lambda e, f=fn, h=inc.h, a=amt: f(e).then_inc(h, a))
        return inc.n

    def mm(self, lhsT, rhs, out, start, stop, inc=None):
        return self.run("tensor", lambda e: e.matmul(out, lhsT, rhs, start=start, stop=stop), inc)

    def tp(self, out_ps, in_, ident, inc=None):
        return self.run("tensor", lambda e: e.transpose(out_ps, in_, ident), inc)

    def cp(self, eng, out, in_, inc=None):
        if eng == "scalar":
            return self.run(eng, lambda e: e.copy(out, in_), inc)
        return self.run(eng, lambda e: e.tensor_copy(out, in_), inc)

    def tt(self, eng, out, a, b, op, inc=None):
        return self.run(eng, lambda e: e.tensor_tensor(out, a, b, op), inc)

    def ts(self, eng, out, in0, s1, s2, op0, op1=None, inc=None):
        if op1 is None:
            return self.run(eng, lambda e: e.tensor_scalar(out, in0, s1, None, op0), inc)
        return self.run(eng, lambda e: e.tensor_scalar(out, in0, s1, s2, op0, op1), inc)

    def stt(self, eng, out, in0, scalar, in1, op0, op1, inc=None):
        return self.run(eng, lambda e: e.scalar_tensor_tensor(out, in0, scalar, in1, op0, op1), inc)

    def act(self, out, in_, func, bias=None, scale=None, accum=None, inc=None):
        kw = {}
        if bias is not None:
            kw["bias"] = bias
        if scale is not None:
            kw["scale"] = scale
        if accum is not None:
            kw["accum_out"] = accum
        return self.run("scalar", lambda e: e.activation(out, in_, func, **kw), inc)

    def rmax(self, out, in_, negate=False, inc=None):
        return self.run("vector", lambda e: e.reduce_max(out, in_, axis=AX.X, negate=negate), inc)

    def rsum(self, out, in_, inc=None):
        return self.run("vector", lambda e: e.reduce_sum(out, in_, axis=AX.X), inc)

    def recip(self, out, in_, inc=None):
        return self.run("vector", lambda e: e.reciprocal(out, in_), inc)

    def dma(self, q, out, in_, sem, accum=False):
        if accum:
            assert q == "gpsimd"
            return self.run(q, lambda e: e.dma_start(out=out, in_=in_, accum_op=ALU.add),
                            sem, 16, serial=False)
        return self.run(q, lambda e: e.dma_start(out=out, in_=in_), sem, 16, serial=False)


def emit_block(nc, prog):
    with nc.Block() as block:
        @block.tensor
        def _(eng):
            for fn in prog.streams["tensor"]:
                fn(eng)

        @block.vector
        def _(eng):
            for fn in prog.streams["vector"]:
                fn(eng)

        @block.scalar
        def _(eng):
            for fn in prog.streams["scalar"]:
                fn(eng)

        @block.gpsimd
        def _(eng):
            for fn in prog.streams["gpsimd"]:
                fn(eng)

        @block.sync
        def _(eng):
            for fn in prog.streams["sync"]:
                fn(eng)


class Core:

    def __init__(self, nc):
        self.nc = nc
        self.p = Prog(nc)
        self.sP = self.p.sem("sP")
        self.sV = self.p.sem("sV")
        self.sA = self.p.sem("sA")
        self.sG = self.p.sem("sG")
        self.dS = self.p.sem("dS")
        self.dC = self.p.sem("dC")
        self.dG = self.p.sem("dG")
        self.cc = self.p.sem("cc")
        self.dram = {}
        self.sb = {}
        self.scopes = {}
        self.pscnt = 0
        self.p.self_sems = {"vector": self.sV, "scalar": self.sA, "gpsimd": self.sG}

    def din(self, name, shape, dtype=F32):
        ap = self.nc.dram_tensor(name, list(shape), dtype, kind="ExternalInput").ap()
        self.dram[name] = ap
        return ap

    def dout(self, name, shape, dtype=F32):
        ap = self.nc.dram_tensor(name, list(shape), dtype, kind="ExternalOutput").ap()
        self.dram[name] = ap
        return ap

    def dint(self, name, shape, dtype=F32):
        ap = self.nc.dram_tensor(name, list(shape), dtype).ap()
        self.dram[name] = ap
        return ap

    def sbuf(self, name, shape, dtype=F32, scope="g"):
        st = self.scopes.setdefault(scope, ExitStack())
        h = st.enter_context(self.nc.sbuf_tensor(name, list(shape), dtype))
        ap = h.ap()
        self.sb[name] = ap
        return ap

    def free_scope(self, scope):
        self.scopes.pop(scope).close()

    def psum(self, name, shape, dtype=F32):
        return self.nc.alloc_psum_tensor(name, list(shape), dtype).ap()

    def cc_fence(self, tag):
        p, cc = self.p, self.cc
        fin = self.dint(f"fence_in_{tag}", (128, 4))
        fout = self.dint(f"fence_out_{tag}", (128, 4))
        p.on("gpsimd", lambda e: e.collective_compute(
            "AllReduce", ALU.add, replica_groups=[[0, 1, 2, 3], [4, 5, 6, 7]],
            ins=[fin.opt()], outs=[fout.opt()]).then_inc(cc.h))
        cc.n += 1
        p.wait("gpsimd", cc)



def build_l0(k: Core, lim=99, sub=99):
    nc, p = k.nc, k.p
    sP, sV, sA, sG, dS, dC, dG, cc = k.sP, k.sV, k.sA, k.sG, k.dS, k.dC, k.dG, k.cc

    xT = k.din("xT", (H, T))
    wqT = k.din("wqT", (H, D))
    wkT = k.din("wkT", (H, D))
    wvT = k.din("wvT", (H, D))
    woT = k.din("woT", (D, H))
    Mm = k.din("Mm", (D, D))
    zv = k.din("zv", (D, 1))
    gate = k.din("gate", (1, 1))
    confWT = k.din("confWT", (H, E8))
    confb = k.din("confb", (E8, 1))
    wealth = k.din("wealth", (E8, 1))
    bgate = k.din("bgate", (H, I0))
    bup = k.din("bup", (H, I0))
    bdown = k.din("bdown", (I0, H))
    Acat = k.din("Acat", (H, 1024))
    gateB = k.din("gateB", (128, 4 * I0))
    upB = k.din("upB", (128, 4 * I0))
    downA = k.din("downA", (I0, 512))
    downB = k.din("downB", (128, 4 * H))
    ident = k.din("ident", (128, 128))
    ones1 = k.din("ones1", (1, 128))
    onescol = k.din("onescol", (128, 1))
    cmaskT_d = k.din("cmaskT", (128, 128))
    pmask8 = k.din("pmask8", (128, 4))
    sel8 = k.din("sel8", (E8, E8 * 128))

    accT = k.dout("accT", (H, T))
    tcon = k.dint("l0_tcon", (128, 132))
    tgat = k.dint("l0_tgat", (512, 132))

    xn = k.sbuf("xn", (128, HC * T))
    cid = k.sbuf("cid", (128, 128))
    c1r = k.sbuf("c1r", (1, 128))
    ccol = k.sbuf("ccol", (128, 1))
    cmask = k.sbuf("cmaskS", (128, 128))
    cpm = k.sbuf("cpm", (128, 4))
    g_s = k.sbuf("g_s", (1, 1))
    omg_s = k.sbuf("omg_s", (1, 1))
    rows = k.sbuf("rows", (1, TP))
    rows2 = k.sbuf("rows2", (1, TP))
    rows3 = k.sbuf("rows3", (1, TP))
    tmp128 = k.sbuf("tmp128", (128, 128))
    stage = k.sbuf("stage", (128, 2 * T))
    tsq = stage[:, 0:2 * TP]
    weT = k.sbuf("weT", (E8, T), scope="l0w")
    csel = k.sbuf("csel", (E8, E8 * 128), scope="l0w")
    wq_s = k.sbuf("wq_s", (128, HC * D), scope="a")
    wk_s = k.sbuf("wk_s", (128, HC * D), scope="a")
    wv_s = k.sbuf("wv_s", (128, HC * D), scope="a")
    wo_s = k.sbuf("wo_s", (128, H), scope="a")
    M_s = k.sbuf("M_s", (128, 128), scope="a")
    z_s = k.sbuf("z_s", (128, 1), scope="a")
    cwT = k.sbuf("cwT", (128, HC * E8), scope="a")
    cb_s = k.sbuf("cb_s", (E8, 1), scope="a")
    wl_s = k.sbuf("wl_s", (E8, 1), scope="a")
    sq_t = k.sbuf("sq_t", (128, T), scope="a")
    pk_t = k.sbuf("pk_t", (128, T), scope="a")
    v_t = k.sbuf("v_t", (128, T), scope="a")
    vT_t = k.sbuf("vT_t", (128, T), scope="a")
    amT_t = k.sbuf("amT_t", (128, T), scope="a")
    dS_t = k.sbuf("dS_t", (128, T), scope="a")
    zc_t = k.sbuf("zc_t", (128, 8), scope="a")
    Tacc = k.sbuf("Tacc", (128, 132), scope="a")
    gath = k.sbuf("gath", (128, 4 * 132), scope="a")
    S_run = k.sbuf("S_run", (128, 128), scope="a")
    z_run = k.sbuf("z_run", (128, 1), scope="a")
    out_sb = k.sbuf("out_sb", (128, T), scope="a")
    bt128 = k.sbuf("bt128", (128, 8), scope="a")
    r128a = k.sbuf("r128a", (128, 8), scope="a")
    r128b = k.sbuf("r128b", (128, 8), scope="a")
    col1 = k.sbuf("col1", (128, 1), scope="a")
    col2 = k.sbuf("col2", (128, 1), scope="a")
    col3 = k.sbuf("col3", (128, 1), scope="a")
    col4 = k.sbuf("col4", (128, 1), scope="a")

    pb = [k.psum(f"pb{i}", (128, 512)) for i in range(8)]

    p.dma("sync", accT, xT, dS)
    acc_init = dS.n
    for c in range(HC):
        p.dma("sync", xn[:, c * T:(c + 1) * T], xT[c * 128:(c + 1) * 128, :], dS)
    ld_x = dS.n
    for dst, src in [(cid, ident), (c1r, ones1), (ccol, onescol), (cmask, cmaskT_d),
                     (cpm, pmask8), (csel, sel8), (wo_s, woT), (M_s, Mm), (z_s, zv),
                     (g_s, gate), (cb_s, confb), (wl_s, wealth)]:
        p.dma("scalar", dst, src, dC)
    for c in range(HC):
        p.dma("scalar", wq_s[:, c * D:(c + 1) * D], wqT[c * 128:(c + 1) * 128, :], dC)
        p.dma("scalar", wk_s[:, c * D:(c + 1) * D], wkT[c * 128:(c + 1) * 128, :], dC)
        p.dma("scalar", wv_s[:, c * D:(c + 1) * D], wvT[c * 128:(c + 1) * 128, :], dC)
        p.dma("scalar", cwT[:, c * E8:(c + 1) * E8], confWT[c * 128:(c + 1) * 128, :], dC)
    ld_small = dC.n

    def xch(c, hf):
        return xn[:, c * T + hf * TP: c * T + (hf + 1) * TP]

    k.l0 = dict(xn=xn, weT=weT, pb=pb, cid=cid, c1r=c1r, ccol=ccol, csel=csel,
                stage=stage, tsq=tsq, accT=accT, acc_region={},
                bgate=bgate, bup=bup, bdown=bdown, Acat=Acat, gateB=gateB,
                upB=upB, downA=downA, downB=downB, xch=xch, rows=rows,
                rows2=rows2, rows3=rows3)
    if lim < 1:
        return k
    p.wait("tensor", dS, ld_x)
    p.wait("tensor", dC, ld_small)
    for hf in range(2):
        for c in range(HC):
            p.mm(ccol, xch(c, hf), pb[6][0:1, :], c == 0, c == HC - 1,
                 sP if c == HC - 1 else None)
        p.wait("vector", sP)
        p.ts("vector", rows, pb[6][0:1, :], 1.0 / H, None, ALU.mult, inc=sV)
        p.wait("tensor", sV)
        p.mm(c1r, rows, pb[7], True, True, sP)
        p.wait("vector", sP)
        for c in range(HC):
            p.tt("vector", xch(c, hf), xch(c, hf), pb[7], ALU.subtract,
                 inc=sV if c == HC - 1 else None)
        sub_done = sV.n
        mm_after = []
        for c in range(HC):
            par = c % 2
            p.wait("scalar", sV, sub_done)
            if c >= 2:
                p.wait("scalar", sP, mm_after[c - 2])
            p.act(tsq[:, par * TP:(par + 1) * TP], xch(c, hf), AF.Square, inc=sA)
            p.wait("tensor", sA)
            p.mm(ccol, tsq[:, par * TP:(par + 1) * TP], pb[6][0:1, :],
                 c == 0, c == HC - 1, sP)
            mm_after.append(sP.n)
        p.wait("vector", sP)
        p.ts("vector", rows, pb[6][0:1, :], 1.0 / H, LNEPS, ALU.mult, ALU.add, inc=sV)
        p.wait("scalar", sV)
        p.act(rows2, rows, AF.Sqrt, inc=sA)
        p.wait("vector", sA)
        p.recip(rows3, rows2, sV)
        p.tt("vector", rows2, rows3, rows3, ALU.mult)
        p.tt("vector", rows2, rows2, rows, ALU.mult)
        p.ts("vector", rows2, rows2, -0.5, 1.5, ALU.mult, ALU.add)
        p.tt("vector", rows3, rows3, rows2, ALU.mult, inc=sV)
        p.wait("tensor", sV)
        p.mm(c1r, rows3, pb[7], True, True, sP)
        p.wait("vector", sP)
        for c in range(HC):
            p.tt("vector", xch(c, hf), xch(c, hf), pb[7], ALU.mult,
                 inc=sV if c == HC - 1 else None)
    xn_ready = sV.n
    if lim < 2:
        return k

    p.wait("scalar", dC, ld_small)
    p.act(g_s, g_s, AF.Sigmoid, inc=sA)
    p.wait("vector", sA)
    p.ts("vector", omg_s, g_s, -1.0, 1.0, ALU.mult, ALU.add, inc=sV)
    if sub < -2:
        return k

    for hf in range(2):
        t0 = hf * TP
        p.wait("tensor", sV, xn_ready)
        for wS, bank in ((wq_s, 0), (wk_s, 1), (wv_s, 2)):
            for c in range(HC):
                p.mm(wS[:, c * D:(c + 1) * D], xch(c, hf), pb[bank],
                     c == 0, c == HC - 1, sP if c == HC - 1 else None)
        p.wait("vector", sP)
        p.wait("scalar", sP)
        if sub < -1:
            return k
        dq = sq_t[:, t0:t0 + TP]
        dk = pk_t[:, t0:t0 + TP]
        p.ts("vector", dq, pb[0], 0.0, None, ALU.min, inc=sV)
        p.act(tsq[:, TP:2 * TP], pb[1], AF.Relu, inc=sA)
        minq = sV.n
        reluk = sA.n
        p.wait("scalar", sV, minq)
        p.act(tsq[:, 0:TP], pb[0], AF.Relu, inc=sA)
        p.wait("vector", sA, reluk)
        p.ts("vector", dk, pb[1], 0.0, None, ALU.min, inc=sV)
        p.wait("scalar", sV)
        for d in (dq, dk):
            p.act(d, d, AF.Exp, inc=sA)
        p.wait("vector", sA)
        p.tt("vector", dq, dq, tsq[:, 0:TP], ALU.add, inc=sV)
        p.tt("vector", dk, dk, tsq[:, TP:2 * TP], ALU.add, inc=sV)
        p.cp("vector", v_t[:, t0:t0 + TP], pb[2], inc=sV)
        sig_done = sV.n
        if sub < 1 + 10 * hf:
            return k

        p.wait("tensor", sV, sig_done)
        p.mm(M_s, sq_t[:, t0:t0 + TP], pb[3], True, True, sP)
        p.mm(z_s, sq_t[:, t0:t0 + TP], pb[4][0:1, :], True, True, sP)
        p.wait("vector", sP)
        p.ts("vector", rows, pb[4][0:1, :], EPS0, None, ALU.add, inc=sV)
        p.recip(rows2, rows, sV)
        p.ts("vector", rows2, rows2, g_s[0:1, 0:1], None, ALU.mult, inc=sV)
        p.wait("tensor", sV)
        p.mm(c1r, rows2, pb[4], True, True, sP)
        p.wait("vector", sP)
        p.cp("vector", out_sb[:, t0:t0 + TP], pb[3], inc=sV)
        p.tt("vector", out_sb[:, t0:t0 + TP], out_sb[:, t0:t0 + TP], pb[4], ALU.mult, inc=sV)
        memo_done = sV.n
        if sub < 2 + 10 * hf:
            return k

        p.wait("tensor", sV, memo_done)
        for c in range(HC):
            p.mm(cwT[:, c * E8:(c + 1) * E8], xch(c, hf), pb[5][0:E8, :],
                 c == 0, c == HC - 1, sP if c == HC - 1 else None)
        p.wait("scalar", sP)
        p.act(tsq[0:E8, 0:TP], pb[5][0:E8, :], AF.Sigmoid, bias=cb_s, inc=sA)
        p.wait("vector", sA)
        p.ts("vector", tsq[0:E8, 0:TP], tsq[0:E8, 0:TP], wl_s, None, ALU.mult, inc=sV)
        if sub < 3 + 10 * hf:
            return k
        for tb in range(TP // 128):
            p.wait("tensor", sV)
            p.tp(pb[6][:, 0:8], tsq[0:E8, tb * 128:(tb + 1) * 128], cid[0:8, 0:8], sP)
            p.wait("vector", sP)
            p.cp("vector", bt128, pb[6][:, 0:8])
            p.rmax(col1, bt128)
            p.ts("vector", r128a, bt128, col1, None, ALU.is_equal)
            p.rsum(col2, r128a)
            p.stt("vector", r128b, r128a, -BIG, bt128, ALU.mult, ALU.add)
            p.rmax(col3, r128b)
            p.ts("vector", col2, col2, 1.5, None, ALU.is_ge)
            p.tt("vector", col4, col1, col3, ALU.subtract)
            p.tt("vector", col4, col4, col2, ALU.mult)
            p.tt("vector", col3, col3, col4, ALU.add)
            p.tt("vector", col4, col1, col3, ALU.subtract, inc=sV)
            p.wait("scalar", sV)
            p.act(col4, col4, AF.Sigmoid, inc=sA)
            p.wait("vector", sA)
            p.ts("vector", r128b, bt128, col3, None, ALU.is_equal)
            p.ts("vector", bt128, r128a, -1.0, 1.0, ALU.mult, ALU.add)
            p.tt("vector", r128b, r128b, bt128, ALU.mult)
            p.ts("vector", r128a, r128a, col4, None, ALU.mult)
            p.ts("vector", col4, col4, -1.0, 1.0, ALU.mult, ALU.add)
            p.ts("vector", r128b, r128b, col4, None, ALU.mult)
            p.tt("vector", r128a, r128a, r128b, ALU.add, inc=sV)
            p.wait("tensor", sV)
            p.tp(pb[6][0:8, 0:128], r128a, cid, sP)
            p.wait("vector", sP)
            p.cp("vector", weT[:, t0 + tb * 128:t0 + (tb + 1) * 128], pb[6][0:8, 0:128], inc=sV)
    qkv_all = sV.n
    if lim < 3:
        return k

    p.wait("tensor", sV, qkv_all)
    for c in range(8):
        csl = slice(c * 128, (c + 1) * 128)
        p.wait("tensor", sV)
        p.tp(pb[0][:, 0:128], v_t[:, csl], cid, sP)
        p.tp(pb[1][:, 0:128], pk_t[:, csl], cid, sP)
        p.mm(pk_t[:, csl], sq_t[:, csl], pb[2][:, 0:128], True, True, sP)
        p.wait("vector", sP)
        p.cp("vector", vT_t[:, csl], pb[0][:, 0:128])
        p.cp("vector", tmp128, pb[1][:, 0:128])
        p.tt("vector", amT_t[:, csl], pb[2][:, 0:128], cmask, ALU.mult)
        p.rsum(zc_t[:, c:c + 1], pk_t[:, csl], inc=sV)
        p.wait("tensor", sV)
        p.mm(tmp128, vT_t[:, csl], pb[3][:, 0:128], True, True, sP)
        p.wait("vector", sP)
        p.cp("vector", dS_t[:, csl], pb[3][:, 0:128], inc=sV)
    p.tt("vector", Tacc[:, 0:128], dS_t[:, 0:128], dS_t[:, 128:256], ALU.add)
    for c in range(2, 8):
        p.tt("vector", Tacc[:, 0:128], Tacc[:, 0:128], dS_t[:, c * 128:(c + 1) * 128], ALU.add)
    p.rsum(Tacc[:, 128:129], zc_t)
    p.ts("vector", Tacc[:, 129:132], Tacc[:, 129:132], 0.0, None, ALU.mult, inc=sV)
    tot_done = sV.n
    if lim < 4:
        return k

    p.wait("gpsimd", sV, tot_done)
    p.dma("gpsimd", tcon, Tacc, dG)
    p.wait("gpsimd", dG)
    p.on("gpsimd", lambda e: e.collective_compute(
        "AllGather", ALU.bypass, replica_groups=[[0, 1, 2, 3], [4, 5, 6, 7]],
        ins=[tcon.opt()], outs=[tgat.opt()]).then_inc(cc.h))
    cc.n += 1
    p.wait("gpsimd", cc)
    k.cc_fence("l0")
    for r in range(4):
        p.dma("gpsimd", gath[:, r * 132:(r + 1) * 132], tgat[r * 128:(r + 1) * 128, :], dG)
    p.wait("vector", dG)
    p.ts("vector", S_run, gath[:, 0:128], cpm[:, 0:1], None, ALU.mult)
    p.ts("vector", z_run, gath[:, 128:129], cpm[:, 0:1], None, ALU.mult)
    for r in range(1, 4):
        p.ts("vector", tmp128, gath[:, r * 132:r * 132 + 128], cpm[:, r:r + 1], None, ALU.mult)
        p.tt("vector", S_run, S_run, tmp128, ALU.add)
        p.ts("vector", tmp128[:, 0:1], gath[:, r * 132 + 128:r * 132 + 129],
             cpm[:, r:r + 1], None, ALU.mult)
        p.tt("vector", z_run, z_run, tmp128[:, 0:1], ALU.add,
             inc=sV if r == 3 else None)
    pref_done = sV.n
    if lim < 5:
        return k

    for c in range(8):
        csl = slice(c * 128, (c + 1) * 128)
        p.wait("tensor", sV)
        p.mm(S_run, sq_t[:, csl], pb[0][:, 0:128], True, False, sP)
        p.mm(vT_t[:, csl], amT_t[:, csl], pb[0][:, 0:128], False, True, sP)
        p.mm(z_run, sq_t[:, csl], pb[1][0:1, 0:128], True, False, sP)
        p.mm(ccol, amT_t[:, csl], pb[1][0:1, 0:128], False, True, sP)
        p.wait("vector", sP)
        p.tt("vector", S_run, S_run, dS_t[:, csl], ALU.add)
        p.tt("vector", z_run, z_run, zc_t[:, c:c + 1], ALU.add)
        p.ts("vector", rows2[0:1, 0:128], pb[1][0:1, 0:128], EPS0, None, ALU.add)
        p.recip(rows3[0:1, 0:128], rows2[0:1, 0:128])
        p.ts("vector", rows3[0:1, 0:128], rows3[0:1, 0:128], omg_s[0:1, 0:1],
             None, ALU.mult, inc=sV)
        p.wait("tensor", sV)
        p.mm(c1r, rows3[0:1, 0:128], pb[2][:, 0:128], True, True, sP)
        p.wait("vector", sP)
        p.cp("vector", tmp128, pb[0][:, 0:128])
        p.tt("vector", tmp128, tmp128, pb[2][:, 0:128], ALU.mult)
        p.tt("vector", out_sb[:, csl], out_sb[:, csl], tmp128, ALU.add, inc=sV)
    walk_done = sV.n
    if lim < 6:
        return k

    p.wait("tensor", sV, walk_done)
    p.wait("gpsimd", dS, acc_init)
    acc_region = {}
    stage_free = {}
    for hc in range(HC):
        par = hc % 2
        for hf in range(2):
            p.wait("tensor", sV)
            p.mm(wo_s[:, hc * 128:(hc + 1) * 128], out_sb[:, hf * TP:(hf + 1) * TP],
                 pb[4 + par], True, True, sP)
            p.wait("vector", sP)
            if hc >= 2 and hf == 0:
                p.wait("vector", dG, stage_free[par])
            p.cp("vector", stage[:, par * T + hf * TP: par * T + (hf + 1) * TP],
                 pb[4 + par], inc=sV)
        p.wait("gpsimd", sV)
        p.dma("gpsimd", accT[hc * 128:(hc + 1) * 128, :],
              stage[:, par * T:(par + 1) * T], dG, accum=True)
        acc_region[hc] = dG.n
        stage_free[par] = dG.n
    attn_acc = dict(acc_region)

    k.l0["acc_region"] = acc_region
    return k



def build_l0_ffn(k: Core):
    nc, p = k.nc, k.p
    sP, sV, sA, sG, dS, dC, dG = k.sP, k.sV, k.sA, k.sG, k.dS, k.dC, k.dG
    L = k.l0
    xn, weT, pb, cid, c1r, ccol, csel = (L["xn"], L["weT"], L["pb"], L["cid"],
                                         L["c1r"], L["ccol"], L["csel"])
    stage, accT, acc_region = L["stage"], L["accT"], L["acc_region"]
    bgate, bup, bdown, Acat = L["bgate"], L["bup"], L["bdown"], L["Acat"]
    gateB, upB, downA, downB = L["gateB"], L["upB"], L["downA"], L["downB"]
    xch = L["xch"]

    p.wait("sync", sP, sP.n)
    p.wait("sync", sV, sV.n)
    p.wait("sync", sA, sA.n)
    p.wait("sync", sG, sG.n)
    p.wait("gpsimd", sP, sP.n)
    p.wait("gpsimd", sV, sV.n)
    p.wait("tensor", sV, sV.n)
    p.wait("vector", sP, sP.n)
    p.wait("scalar", sV, sV.n)
    k.free_scope("a")
    xaa = k.sbuf("xaa", (128, 8 * TP), scope="f")
    WE = [k.sbuf(f"WE{e}", (128, TP), scope="f") for e in range(E8)]
    ha = k.sbuf("ha", (128, 4 * T), scope="f")
    hbar = k.sbuf("hbar", (128, GRP * TP), scope="f")
    bg_sb = k.sbuf("bg_sb", (128, TP), scope="f")
    bu_sb = k.sbuf("bu_sb", (128, TP), scope="f")
    tg_sb = k.sbuf("tg_sb", (128, TP), scope="f")
    tu_sb = k.sbuf("tu_sb", (128, TP), scope="f")
    sgw_sb = k.sbuf("sgw_sb", (128, TP), scope="f")
    wgat = [k.sbuf(f"wgat{i}", (128, HC * 128), scope="fw") for i in range(2)]
    wup = [k.sbuf(f"wup{i}", (128, HC * 128), scope="fw") for i in range(2)]
    wgB = [k.sbuf("wgB0", (128, 4 * 128), scope="fw")]
    wuB = [k.sbuf("wuB0", (128, 4 * 128), scope="fw")]
    wdA = [k.sbuf("wdA0", (128, 512), scope="fw")]
    wdn = [k.sbuf("wdn0", (128, GRP * H), scope="fw")]

    gB4 = gateB.rearrange("p (f i) -> p f i", f=4)
    uB4 = upB.rearrange("p (f i) -> p f i", f=4)
    dB4 = downB.rearrange("p (f h) -> p f h", f=4)
    wgB4 = [w.rearrange("p (f i) -> p f i", f=4) for w in wgB]
    wuB4 = [w.rearrange("p (f i) -> p f i", f=4) for w in wuB]

    pe_free = {}
    buffree = {}
    dve_free = {}

    for pf in range(2):
        t0 = pf * TP

        for c8 in range(8):
            par = c8 % 2
            if ("wgat", par) in buffree:
                p.wait("sync", sP, buffree[("wgat", par)])
            for c in range(HC):
                p.dma("sync", wgat[par][:, c * 128:(c + 1) * 128],
                      Acat[c * 128:(c + 1) * 128, c8 * 128:(c8 + 1) * 128], dS)
            p.wait("tensor", dS)
            p.wait("tensor", sV)
            for c in range(HC):
                p.mm(wgat[par][:, c * 128:(c + 1) * 128], xch(c, pf), pb[0],
                     c == 0, c == HC - 1, sP if c == HC - 1 else None)
            buffree[("wgat", par)] = sP.n
            p.wait("vector", sP)
            p.cp("vector", xaa[:, c8 * TP:(c8 + 1) * TP], pb[0], inc=sV)

        for e in range(E8):
            p.wait("tensor", sV)
            p.mm(csel[:, e * 128:(e + 1) * 128], weT[:, t0:t0 + TP], pb[1], True, True, sP)
            p.wait("vector", sP)
            p.cp("vector", WE[e], pb[1], inc=sV)

        for f in range(4):
            p.run("gpsimd", lambda e, f=f, t0=t0: e.memset(
                ha[:, f * T + t0: f * T + t0 + TP], 0.0), sG)
        ha_zeroed = sG.n

        down_add_count = dict(acc_region) if pf == 0 else dict(k._ffn_regions)
        for ic in range(IC0):
            par = ic % 2
            gpar = (ic // GRP) % 2
            icg = ic % GRP
            g = ic // GRP
            if ("wgat", par) in buffree:
                p.wait("sync", sP, buffree[("wgat", par)])
            for c in range(HC):
                p.dma("sync", wgat[par][:, c * 128:(c + 1) * 128],
                      bgate[c * 128:(c + 1) * 128, ic * 128:(ic + 1) * 128], dS)
            if ("wup", par) in buffree:
                p.wait("sync", sP, buffree[("wup", par)])
            for c in range(HC):
                p.dma("sync", wup[par][:, c * 128:(c + 1) * 128],
                      bup[c * 128:(c + 1) * 128, ic * 128:(ic + 1) * 128], dS)
            if ("wB", 0) in buffree:
                p.wait("sync", sP, buffree[("wB", 0)])
            p.dma("sync", wgB4[0], gB4[:, :, ic * 128:(ic + 1) * 128], dS)
            p.dma("sync", wuB4[0], uB4[:, :, ic * 128:(ic + 1) * 128], dS)
            p.dma("sync", wdA[0], downA[ic * 128:(ic + 1) * 128, :], dS)
            if icg == 0 and ("wdn", 0) in buffree:
                p.wait("sync", sP, buffree[("wdn", 0)])
            if icg == 0:
                for i2 in range(GRP):
                    p.dma("sync", wdn[0][:, i2 * H:(i2 + 1) * H],
                          bdown[(ic + i2) * 128:(ic + i2 + 1) * 128, :], dS)
            wload = dS.n

            p.wait("tensor", dS, wload)
            p.wait("tensor", sV)
            if pf == 0 and ic == 0:
                p.wait("gpsimd", sG, ha_zeroed)
            for c in range(HC):
                p.mm(wgat[par][:, c * 128:(c + 1) * 128], xch(c, pf), pb[0],
                     c == 0, c == HC - 1, sP if c == HC - 1 else None)
            gmm = sP.n
            buffree[("wgat", par)] = gmm
            for c in range(HC):
                p.mm(wup[par][:, c * 128:(c + 1) * 128], xch(c, pf), pb[1],
                     c == 0, c == HC - 1, sP if c == HC - 1 else None)
            umm = sP.n
            buffree[("wup", par)] = umm
            p.wait("vector", sP, gmm)
            p.cp("vector", bg_sb, pb[0], inc=sV)
            p.wait("vector", sP, umm)
            p.cp("vector", bu_sb, pb[1], inc=sV)
            base_ev = sV.n

            for e in range(E8):
                f = e // 2
                r0 = (e % 2) * 64
                lgb = pb[2 + (e % 2)]
                lub = pb[4 + (e % 2)]
                hab = pb[6 + (f % 2)]
                p.wait("tensor", sV)
                p.mm(wgB[0][r0:r0 + 64, f * 128:(f + 1) * 128],
                     xaa[r0:r0 + 64, f * TP:(f + 1) * TP] if pf == 0 else
                     xaa[r0:r0 + 64, f * TP:(f + 1) * TP],
                     lgb, True, True, sP)
                p.mm(wuB[0][r0:r0 + 64, f * 128:(f + 1) * 128],
                     xaa[r0:r0 + 64, (4 + f) * TP:(5 + f) * TP],
                     lub, True, True, sP)
                lora_done = sP.n
                p.wait("vector", sP, lora_done)
                p.stt("vector", tg_sb, lgb, SCALING, bg_sb, ALU.mult, ALU.add, inc=sV)
                p.stt("vector", tu_sb, lub, SCALING, bu_sb, ALU.mult, ALU.add, inc=sV)
                tgu = sV.n
                p.wait("scalar", sV, tgu)
                p.act(sgw_sb, tg_sb, AF.Silu, inc=sA)
                p.wait("gpsimd", sA)
                p.tt("gpsimd", sgw_sb, sgw_sb, WE[e], ALU.mult, inc=sG)
                p.wait("vector", sG)
                p.tt("vector", tg_sb, sgw_sb, tu_sb, ALU.mult, inc=sV)
                hw_done = sV.n
                p.wait("gpsimd", sV, hw_done)
                if e == 0:
                    p.cp("gpsimd", hbar[:, icg * TP:(icg + 1) * TP], tg_sb, inc=sG)
                else:
                    p.tt("gpsimd", hbar[:, icg * TP:(icg + 1) * TP],
                         hbar[:, icg * TP:(icg + 1) * TP], tg_sb, ALU.add, inc=sG)
                p.wait("tensor", sV, hw_done)
                if e % 2 == 0:
                    p.wait("tensor", sV)
                    p.run("tensor", lambda en, w=wdA[0], e=e, b=hab: en.matmul(
                        b[0:64, :], w[:, e * 64:(e + 1) * 64], tg_sb,
                        start=True, stop=True), sP)
                else:
                    p.run("tensor", lambda en, w=wdA[0], e=e, b=hab: en.matmul(
                        b[64:128, :], w[:, e * 64:(e + 1) * 64], tg_sb,
                        start=True, stop=True, tile_position=(0, 64)), sP)
                    hamm = sP.n
                    p.wait("vector", sP, hamm)
                    hsl = ha[:, f * T + t0: f * T + t0 + TP]
                    p.tt("vector", hsl, hsl, hab, ALU.add, inc=sV)
            buffree[("wB", 0)] = sP.n

            if icg == GRP - 1:
                p.wait("tensor", sG)
                for hc in range(HC):
                    dpar = hc % 2
                    p.wait("tensor", sV)
                    for i2 in range(GRP):
                        p.mm(wdn[0][:, i2 * H + hc * 128: i2 * H + (hc + 1) * 128],
                             hbar[:, i2 * TP:(i2 + 1) * TP], pb[2 + dpar],
                             i2 == 0, i2 == GRP - 1, sP if i2 == GRP - 1 else None)
                    dmm = sP.n
                    p.wait("vector", sP, dmm)
                    skey = ("dstage", dpar)
                    if skey in dve_free:
                        p.wait("vector", dG, dve_free[skey])
                    p.cp("vector", stage[:, dpar * T:dpar * T + TP], pb[2 + dpar], inc=sV)
                    sev = sV.n
                    p.wait("gpsimd", sV, sev)
                    p.wait("gpsimd", dG, down_add_count.get(hc, 0))
                    p.dma("gpsimd", accT[hc * 128:(hc + 1) * 128, t0:t0 + TP],
                          stage[:, dpar * T:dpar * T + TP], dG, accum=True)
                    down_add_count[hc] = dG.n
                    dve_free[skey] = dG.n
                buffree[("wdn", 0)] = sP.n
        k._ffn_regions = dict(down_add_count)

    p.wait("sync", sP, sP.n)
    p.wait("sync", sV, sV.n)
    p.wait("sync", sG, sG.n)
    k.free_scope("fw")
    wdB = k.sbuf("wdB", (128, 4 * H), scope="fb")
    wdB4 = wdB.rearrange("p (f h) -> p f h", f=4)
    p.dma("sync", wdB4, dB4, dS)
    p.wait("tensor", dS)
    regions = dict(k._ffn_regions)
    for hc in range(HC):
        for hf in range(2):
            t0 = hf * TP
            dpar = hc % 2
            p.wait("tensor", sV)
            for f in range(4):
                p.mm(wdB[:, f * H + hc * 128: f * H + (hc + 1) * 128],
                     ha[:, f * T + t0: f * T + t0 + TP], pb[4 + dpar],
                     f == 0, f == 3, sP if f == 3 else None)
            bmm = sP.n
            p.wait("vector", sP, bmm)
            skey = ("bstage", dpar, hf)
            if skey in dve_free:
                p.wait("vector", dG, dve_free[skey])
            p.ts("vector", stage[:, dpar * T + t0: dpar * T + t0 + TP],
                 pb[4 + dpar], SCALING, None, ALU.mult, inc=sV)
            p.wait("gpsimd", sV)
            p.wait("gpsimd", dG, regions.get(hc, 0))
            p.dma("gpsimd", accT[hc * 128:(hc + 1) * 128, t0:t0 + TP],
                  stage[:, dpar * T + t0: dpar * T + t0 + TP], dG, accum=True)
            regions[hc] = dG.n
            dve_free[skey] = dG.n
    k.l0_final_regions = regions
    return k


def finish(k: Core):
    p = k.p
    p.wait("gpsimd", k.dG, k.dG.n)
    p.wait("gpsimd", k.dS, k.dS.n)
    p.wait("gpsimd", k.dC, k.dC.n)



def _cc(a):
    return np.ascontiguousarray(a, dtype=np.float32)


def l0_host_inputs(full, core):
    b = core // 4
    s0 = (core % 4) * T
    x = full["L0_attn_hidden_states"][b, s0:s0 + T, :]
    d = {}
    d["xT"] = _cc(x.T)
    d["wqT"] = _cc(full["L0_attn_w_q"].T)
    d["wkT"] = _cc(full["L0_attn_w_k"].T)
    d["wvT"] = _cc(full["L0_attn_w_v"].T)
    d["woT"] = _cc(full["L0_attn_w_o"].T)
    d["Mm"] = _cc(full["L0_attn_M"])
    d["zv"] = _cc(full["L0_attn_z"].reshape(D, 1))
    d["gate"] = _cc(full["L0_attn_gate"].reshape(1, 1))
    d["confWT"] = _cc(full["L0_ffn_conf_W"].T)
    d["confb"] = _cc(full["L0_ffn_conf_b"].reshape(E8, 1))
    d["wealth"] = _cc(full["L0_ffn_expert_wealth"].reshape(E8, 1))
    d["bgate"] = _cc(full["L0_ffn_base_gate"])
    d["bup"] = _cc(full["L0_ffn_base_up"])
    d["bdown"] = _cc(full["L0_ffn_base_down"])
    ga = full["L0_ffn_gate_A"]
    ua = full["L0_ffn_up_A"]
    d["Acat"] = _cc(np.concatenate([ga[e] for e in range(E8)] +
                                   [ua[e] for e in range(E8)], axis=1))
    gB = np.asarray(full["L0_ffn_gate_B"]).reshape(512, I0)
    uB = np.asarray(full["L0_ffn_up_B"]).reshape(512, I0)
    dB = np.asarray(full["L0_ffn_down_B"]).reshape(512, H)
    d["gateB"] = _cc(gB.reshape(4, 128, I0).transpose(1, 0, 2).reshape(128, 4 * I0))
    d["upB"] = _cc(uB.reshape(4, 128, I0).transpose(1, 0, 2).reshape(128, 4 * I0))
    d["downB"] = _cc(dB.reshape(4, 128, H).transpose(1, 0, 2).reshape(128, 4 * H))
    dA = np.asarray(full["L0_ffn_down_A"])
    d["downA"] = _cc(np.concatenate([dA[e] for e in range(E8)], axis=1))
    d["ident"] = np.eye(128, dtype=np.float32)
    d["ones1"] = np.ones((1, 128), dtype=np.float32)
    d["onescol"] = np.ones((128, 1), dtype=np.float32)
    u = np.arange(128)
    d["cmaskT"] = _cc((u[:, None] <= u[None, :]).astype(np.float32))
    pm = np.zeros((128, 4), dtype=np.float32)
    pm[:, :core % 4] = 1.0
    d["pmask8"] = pm
    sel = np.zeros((E8, E8 * 128), dtype=np.float32)
    for e in range(E8):
        sel[e, e * 128:(e + 1) * 128] = 1.0
    d["sel8"] = sel
    return d


def build_l0_program(lim=99, sub=99):
    nc = bass.Bass("TRN2", target_bir_lowering=False)
    k = Core(nc)
    build_l0(k, lim=lim, sub=sub)
    if lim >= 7:
        build_l0_ffn(k)
    finish(k)
    emit_block(nc, k.p)
    return nc


def kernel(**inputs):
    raise NotImplementedError("L1 under construction")



def build_l1(k: Core, fused: bool, lim=99):
    nc, p = k.nc, k.p
    sP, sV, sA, sG, dS, dC, dG, cc = k.sP, k.sV, k.sA, k.sG, k.dS, k.dC, k.dG, k.cc

    if fused:
        accT = k.l0["accT"]
        x1src = accT
        xn = k.l0["xn"]
        cid, c1r, ccol = k.l0["cid"], k.l0["c1r"], k.l0["ccol"]
        stage = k.l0["stage"]
        rows, rows2, rows3 = k.l0["rows"], k.l0["rows2"], k.l0["rows3"]
        p.wait("sync", dG, dG.n)
        p.wait("sync", sP, sP.n)
        p.wait("sync", sV, sV.n)
        p.wait("sync", sG, sG.n)
        p.wait("scalar", dG, dG.n)
        p.wait("gpsimd", dG, dG.n)
        p.wait("tensor", dG, dG.n)
        p.wait("vector", dG, dG.n)
        for sc in ("fb", "f", "l0w"):
            if sc in k.scopes:
                k.free_scope(sc)
    else:
        x1src = k.din("x1T", (H, T))
        accT = k.dout("accT", (H, T))
        xn = k.sbuf("xn", (128, HC * T))
        cid = k.sbuf("cid", (128, 128))
        c1r = k.sbuf("c1r", (1, 128))
        ccol = k.sbuf("ccol", (128, 1))
        rows = k.sbuf("rows", (1, TP))
        rows2 = k.sbuf("rows2", (1, TP))
        rows3 = k.sbuf("rows3", (1, TP))
        stage = k.sbuf("stage", (128, 2 * T))
        ident = k.din("ident", (128, 128))
        ones1 = k.din("ones1", (1, 128))
        onescol = k.din("onescol", (128, 1))
        p.dma("scalar", cid, ident, dC)
        p.dma("scalar", c1r, ones1, dC)
        p.dma("scalar", ccol, onescol, dC)
        p.dma("sync", accT, x1src, dS)

    wqT = k.din("l1wqT", (H, H))
    wkT = k.din("l1wkT", (H, H))
    wvT = k.din("l1wvT", (H, H))
    woT = k.din("l1woT", (H, H))
    W0T = k.din("W0T", (H, N1))
    W1T = k.din("W1T", (N1, H))
    b0 = k.din("b0", (128, N1 // 128))
    b1 = k.din("b1", (128, HC))
    cosT = k.din("cosT", (64, T))
    sinT = k.din("sinT", (64, T))
    perm64 = k.din("perm64", (64, 64))

    conK = k.dint("l1_conK", (H, T))
    gatK = k.dint("l1_gatK", (4 * H, T))
    conV = k.dint("l1_conV", (T, H))
    gatV = k.dint("l1_gatV", (4 * T, H))

    acc_init_cnt = dS.n
    accr = dict(getattr(k, "l0_final_regions", {}))

    pb = k.l0["pb"] if fused else [k.psum(f"pb{i}", (128, 512)) for i in range(8)]

    b0_s = k.sbuf("b0_s", (128, N1 // 128), scope="h0")
    b1_s = k.sbuf("b1_s", (128, HC), scope="h0")
    perm_s = k.sbuf("perm_s", (64, 64), scope="h0")
    cos_s = k.sbuf("cos_s", (64, T), scope="h")
    sin_s = k.sbuf("sin_s", (64, T), scope="h")
    wsl = [k.sbuf(f"wsl{i}", (128, HC * 128), scope="h") for i in range(2)]
    kr_sb = k.sbuf("kr_sb", (128, TP), scope="h")
    tmpA = k.sbuf("tmpA", (64, TP), scope="h")
    tmpB = k.sbuf("tmpB", (64, TP), scope="h")

    p.dma("scalar", cos_s, cosT, dC)
    p.dma("scalar", sin_s, sinT, dC)
    p.dma("scalar", perm_s, perm64, dC)
    p.dma("scalar", b0_s, b0, dC)
    p.dma("scalar", b1_s, b1, dC)
    ld_tab = dC.n

    if fused:
        for c in range(HC):
            p.dma("sync", xn[:, c * T:(c + 1) * T], x1src[c * 128:(c + 1) * 128, :], dS)
    else:
        for c in range(HC):
            p.dma("sync", xn[:, c * T:(c + 1) * T], x1src[c * 128:(c + 1) * 128, :], dS)
    ld_x1 = dS.n

    def xch(c, hf):
        return xn[:, c * T + hf * TP: c * T + (hf + 1) * TP]

    p.wait("tensor", dS, ld_x1)
    p.wait("tensor", dC, ld_tab)
    tsq2 = stage[:, 0:2 * TP]
    for hf in range(2):
        for c in range(HC):
            p.mm(ccol, xch(c, hf), pb[6][0:1, :], c == 0, c == HC - 1,
                 sP if c == HC - 1 else None)
        p.wait("vector", sP)
        p.ts("vector", rows, pb[6][0:1, :], 1.0 / H, None, ALU.mult, inc=sV)
        p.wait("tensor", sV)
        p.mm(c1r, rows, pb[7], True, True, sP)
        p.wait("vector", sP)
        for c in range(HC):
            p.tt("vector", xch(c, hf), xch(c, hf), pb[7], ALU.subtract,
                 inc=sV if c == HC - 1 else None)
        sub_done = sV.n
        mm_after = []
        for c in range(HC):
            par = c % 2
            p.wait("scalar", sV, sub_done)
            if c >= 2:
                p.wait("scalar", sP, mm_after[c - 2])
            p.act(tsq2[:, par * TP:(par + 1) * TP], xch(c, hf), AF.Square, inc=sA)
            p.wait("tensor", sA)
            p.mm(ccol, tsq2[:, par * TP:(par + 1) * TP], pb[6][0:1, :],
                 c == 0, c == HC - 1, sP)
            mm_after.append(sP.n)
        p.wait("vector", sP)
        p.ts("vector", rows, pb[6][0:1, :], 1.0 / H, LNEPS, ALU.mult, ALU.add, inc=sV)
        p.wait("scalar", sV)
        p.act(rows2, rows, AF.Sqrt, inc=sA)
        p.wait("vector", sA)
        p.recip(rows3, rows2, sV)
        p.tt("vector", rows2, rows3, rows3, ALU.mult)
        p.tt("vector", rows2, rows2, rows, ALU.mult)
        p.ts("vector", rows2, rows2, -0.5, 1.5, ALU.mult, ALU.add)
        p.tt("vector", rows3, rows3, rows2, ALU.mult, inc=sV)
        p.wait("tensor", sV)
        p.mm(c1r, rows3, pb[7], True, True, sP)
        p.wait("vector", sP)
        for c in range(HC):
            p.tt("vector", xch(c, hf), xch(c, hf), pb[7], ALU.mult,
                 inc=sV if c == HC - 1 else None)
    xn1_ready = sV.n
    if lim < 2:
        return k

    def rope(src_sb, dst, hf, bankp):
        t0 = hf * TP
        p.wait("tensor", sV)
        p.mm(perm_s, src_sb[0:64, :], bankp[0:64, :], True, True, sP)
        p.tt("vector", tmpA[0:64, :], src_sb[0:64, :], cos_s[:, t0:t0 + TP], ALU.mult)
        p.wait("vector", sP)
        p.tt("vector", tmpB[0:64, :], bankp[0:64, :], sin_s[:, t0:t0 + TP], ALU.mult)
        p.tt("vector", dst[0:64, :], tmpA[0:64, :], tmpB[0:64, :], ALU.add)
        p.cp("vector", dst[64:128, :], src_sb[64:128, :], inc=sV)

    wfree = {}
    for hf in range(2):
        t0 = hf * TP
        for dc in range(HC):
            par = dc % 2
            key = ("wsl", par)
            if key in wfree:
                p.wait("sync", sP, wfree[key])
            for c in range(HC):
                p.dma("sync", wsl[par][:, c * 128:(c + 1) * 128],
                      wkT[c * 128:(c + 1) * 128, dc * 128:(dc + 1) * 128], dS)
            wk_ld = dS.n
            p.wait("tensor", dS, wk_ld)
            p.wait("tensor", sV)
            for c in range(HC):
                p.mm(wsl[par][:, c * 128:(c + 1) * 128], xch(c, hf), pb[0 + par],
                     c == 0, c == HC - 1, sP if c == HC - 1 else None)
            p.wait("vector", sP)
            p.cp("vector", stage[:, 0:TP], pb[0 + par], inc=sV)
            rope(stage[:, 0:TP], kr_sb, hf, pb[2])
            p.wait("gpsimd", sV)
            p.dma("gpsimd", conK[dc * 128:(dc + 1) * 128, t0:t0 + TP], kr_sb, dG)
            wfree[key] = sP.n
        for dc in range(HC):
            par = dc % 2
            key = ("wsl", par)
            p.wait("sync", sP, wfree[key])
            for c in range(HC):
                p.dma("sync", wsl[par][:, c * 128:(c + 1) * 128],
                      wvT[c * 128:(c + 1) * 128, dc * 128:(dc + 1) * 128], dS)
            wv_ld = dS.n
            p.wait("tensor", dS, wv_ld)
            p.wait("tensor", sV)
            for c in range(HC):
                p.mm(wsl[par][:, c * 128:(c + 1) * 128], xch(c, hf), pb[0 + par],
                     c == 0, c == HC - 1, sP if c == HC - 1 else None)
            vmm = sP.n
            wfree[key] = vmm
            p.wait("vector", sP, vmm)
            p.wait("vector", dG, dG.n)
            p.cp("vector", stage[:, 0:TP], pb[0 + par], inc=sV)
            for tb in range(TP // 128):
                p.wait("tensor", sV)
                p.tp(pb[2 + tb % 2][:, 0:128],
                     stage[:, tb * 128:(tb + 1) * 128], cid, sP)
                p.wait("vector", sP)
                p.cp("vector", stage[:, T + tb * 128: T + (tb + 1) * 128],
                     pb[2 + tb % 2][:, 0:128], inc=sV)
                p.wait("gpsimd", sV)
                p.dma("gpsimd",
                      conV[t0 + tb * 128: t0 + (tb + 1) * 128, dc * 128:(dc + 1) * 128],
                      stage[:, T + tb * 128: T + (tb + 1) * 128], dG)
    kv_stores = dG.n
    if lim < 3:
        return k

    p.wait("gpsimd", dG, kv_stores)
    p.on("gpsimd", lambda e: e.collective_compute(
        "AllGather", ALU.bypass, replica_groups=[[0, 1, 2, 3], [4, 5, 6, 7]],
        ins=[conK.opt()], outs=[gatK.opt()]).then_inc(cc.h))
    cc.n += 1
    p.on("gpsimd", lambda e: e.collective_compute(
        "AllGather", ALU.bypass, replica_groups=[[0, 1, 2, 3], [4, 5, 6, 7]],
        ins=[conV.opt()], outs=[gatV.opt()]).then_inc(cc.h))
    cc.n += 1
    p.wait("gpsimd", cc)
    k.cc_fence("l1")
    p.run("gpsimd", lambda e: e.nop(), sG, serial=False)
    cc_done = sG.n
    if CC_DELAY:
        p.wait("tensor", sG, cc_done)
        for _ in range(CC_DELAY):
            p.mm(cid, cid, pb[3][:, 0:128], True, True, None)
        p.run("tensor", lambda e: e.nop(), sP, serial=False)
        cc_done_p = sP.n
        p.wait("scalar", sP, cc_done_p)
        p.wait("sync", sP, cc_done_p)
    if lim < 4:
        return k

    p.wait("sync", sP, sP.n)
    p.wait("vector", dG, dG.n)
    p.wait("sync", sV, sV.n)
    p.wait("sync", sA, sA.n)
    p.wait("sync", sG, sG.n)
    k.free_scope("h")
    regions = accr
    stg_free = {}
    if not fused:
        p.wait("gpsimd", dS, acc_init_cnt)
    Xf = k.sbuf("Xf", (128, HC * TP), scope="n")
    Df = k.sbuf("Df", (128, 16 * TP), scope="n")
    Ea = k.sbuf("Ea", (128, HC * TP), scope="n")
    h16 = k.sbuf("h16", (128, TP), F16, scope="n")
    w0b = [k.sbuf(f"w0b{i}", (128, HC * 128), scope="n") for i in range(2)]
    w1b = [k.sbuf("w1b0", (128, 16 * 128), scope="n")]

    for pf in range(2):
        t0 = pf * TP
        for c in range(HC):
            p.wait("scalar", sV)
            p.act(h16, xch(c, pf), AF.Copy, inc=sA)
            p.wait("vector", sA)
            p.cp("vector", Xf[:, c * TP:(c + 1) * TP], h16, inc=sV)
        w0free = {}
        w1free = {}
        efree = {}
        for g in range(4):
            for ncc in range(16):
                n_i = g * 16 + ncc
                par = n_i % 2
                key = ("w0", par)
                if key in w0free:
                    p.wait("sync", sP, w0free[key])
                for c in range(HC):
                    p.dma("sync", w0b[par][:, c * 128:(c + 1) * 128],
                          W0T[c * 128:(c + 1) * 128, n_i * 128:(n_i + 1) * 128], dS)
                w0_ld = dS.n
                p.wait("tensor", dS, w0_ld)
                p.wait("tensor", sA)
                for c in range(HC):
                    p.mm(w0b[par][:, c * 128:(c + 1) * 128], Xf[:, c * TP:(c + 1) * TP],
                         pb[0 + par], c == 0, c == HC - 1, sP if c == HC - 1 else None)
                w0free[key] = sP.n
                p.wait("scalar", sP)
                p.wait("scalar", sV)
                p.act(h16, pb[0 + par], AF.Gelu, bias=b0_s[:, n_i:n_i + 1], inc=sA)
                p.wait("vector", sA)
                p.cp("vector", Df[:, ncc * TP:(ncc + 1) * TP], h16, inc=sV)
            for hc in range(HC):
                par2 = 0
                key = ("w1", par2)
                if key in w1free:
                    p.wait("sync", sP, w1free[key])
                for c in range(16):
                    p.dma("sync", w1b[par2][:, c * 128:(c + 1) * 128],
                          W1T[(g * 16 + c) * 128:(g * 16 + c + 1) * 128,
                              hc * 128:(hc + 1) * 128], dS)
                w1_ld = dS.n
                p.wait("tensor", dS, w1_ld)
                p.wait("tensor", sV)
                for c in range(16):
                    p.mm(w1b[par2][:, c * 128:(c + 1) * 128], Df[:, c * TP:(c + 1) * TP],
                         pb[4 + par2], c == 0, c == 15, sP if c == 15 else None)
                w1free[key] = sP.n
                p.wait("vector", sP)
                esl = Ea[:, hc * TP:(hc + 1) * TP]
                if g == 0:
                    p.cp("vector", esl, pb[4 + par2], inc=sV)
                else:
                    p.tt("vector", esl, esl, pb[4 + par2], ALU.add, inc=sV)
        for hc in range(HC):
            p.ts("vector", h16, Ea[:, hc * TP:(hc + 1) * TP],
                 b1_s[:, hc:hc + 1], None, ALU.add, inc=sV)
            skey = ("stage", hc % 2)
            if skey in stg_free:
                p.wait("vector", dG, stg_free[skey])
            p.cp("vector", stage[:, (hc % 2) * T:(hc % 2) * T + TP], h16, inc=sV)
            p.wait("gpsimd", sV)
            p.wait("gpsimd", dG, regions.get(hc, 0))
            p.dma("gpsimd", accT[hc * 128:(hc + 1) * 128, t0:t0 + TP],
                  stage[:, (hc % 2) * T:(hc % 2) * T + TP], dG, accum=True)
            regions[hc] = dG.n
            stg_free[skey] = dG.n
    if lim < 5:
        return k

    p.wait("sync", sP, sP.n)
    p.wait("sync", sV, sV.n)
    p.wait("sync", sA, sA.n)
    p.wait("sync", sG, sG.n)
    k.free_scope("n")
    cos_s = k.sbuf("cos2_s", (64, T), scope="h2")
    sin_s = k.sbuf("sin2_s", (64, T), scope="h2")
    tmpA = k.sbuf("tmpA2", (64, TP), scope="h2")
    tmpB = k.sbuf("tmpB2", (64, TP), scope="h2")
    wsl2 = k.sbuf("wsl2", (128, HC * 128), scope="h2")
    p.dma("scalar", cos_s, cosT, dC)
    p.dma("scalar", sin_s, sinT, dC)
    tab2 = dC.n

    def rope2(src_sb, dst, hf, bankp):
        p.wait("tensor", sV)
        p.wait("tensor", dC, tab2)
        p.mm(perm_s, src_sb[0:64, :], bankp[0:64, :], True, True, sP)
        p.tt("vector", tmpA[0:64, :], src_sb[0:64, :], cos_s[:, hf * TP:(hf + 1) * TP], ALU.mult)
        p.wait("vector", sP)
        p.tt("vector", tmpB[0:64, :], bankp[0:64, :], sin_s[:, hf * TP:(hf + 1) * TP], ALU.mult)
        p.tt("vector", dst[0:64, :], tmpA[0:64, :], tmpB[0:64, :], ALU.add)
        p.cp("vector", dst[64:128, :], src_sb[64:128, :], inc=sV)

    q_sb = k.sbuf("q_sb", (128, TP), scope="h2")
    K_sb = k.sbuf("K_sb", (128, 4 * T), scope="h2")
    V_sb = k.sbuf("V_sb", (128, 32 * 128), scope="h2")
    P_sb = k.sbuf("P_sb", (128, 4 * T), scope="h2")
    y_sb = k.sbuf("y_sb", (128, NH * TP), scope="h2")
    PT_sb = [k.sbuf(f"PT{i}", (128, 128), scope="h2") for i in range(2)]
    m_run = k.sbuf("m_run", (128, 2), scope="h2")
    l_run = k.sbuf("l_run", (128, 2), scope="h2")
    wosl = k.sbuf("wosl", (128, HC * 128), scope="h2")
    V3 = V_sb.rearrange("p (b e) -> p b e", b=32)
    gatV3 = gatV.rearrange("(b p) e -> p b e", p=128)

    qfree = 0
    kvfree = 0
    wofree = 0
    stg_free = {}
    for pf in range(2):
        t0 = pf * TP
        for hd in range(NH):
            if hd == 0:
                p.wait("scalar", dG, dG.n)
                p.wait("vector", dG, dG.n)
            if qfree:
                p.wait("sync", sP, qfree)
            for c in range(HC):
                p.dma("sync", wsl2[:, c * 128:(c + 1) * 128],
                      wqT[c * 128:(c + 1) * 128, hd * 128:(hd + 1) * 128], dS)
            q_ld = dS.n
            if kvfree:
                p.wait("scalar", sP, kvfree)
            p.wait("scalar", sG, cc_done)
            for r in range(4):
                p.dma("scalar", K_sb[:, r * T:(r + 1) * T],
                      gatK[r * H + hd * 128: r * H + (hd + 1) * 128, :], dC)
            p.dma("scalar", V3, gatV3[:, :, hd * 128:(hd + 1) * 128], dC)
            kv_ld = dC.n

            p.wait("tensor", dS, q_ld)
            p.wait("tensor", sV)
            for c in range(HC):
                p.mm(wsl2[:, c * 128:(c + 1) * 128], xch(c, pf), pb[0],
                     c == 0, c == HC - 1, sP if c == HC - 1 else None)
            qfree = sP.n
            p.wait("scalar", sP)
            p.act(stage[:, 0:TP], pb[0], AF.Copy, scale=float(1.0 / np.sqrt(HD)), inc=sA)
            p.wait("vector", sA)
            rope2(stage[:, 0:TP], q_sb, pf, pb[1])

            p.wait("tensor", dC, kv_ld)
            for qt in range(4):
                if SYNC_HEAVY:
                    for een in ("tensor", "vector", "scalar", "gpsimd", "sync"):
                        for ss in (sP, sV, sA, sG):
                            p.wait(een, ss, ss.n)
                qsl = q_sb[:, qt * 128:(qt + 1) * 128]
                p.ts("vector", l_run[:, 0:1], l_run[:, 0:1], 0.0, None, ALU.mult)
                for jb in range(8):
                    bank = pb[4 + jb % 2]
                    p.wait("tensor", sV)
                    p.mm(qsl, K_sb[:, jb * 512:(jb + 1) * 512], bank, True, True, sP)
                    p.wait("vector", sP)
                    if jb == 0:
                        p.rmax(m_run[:, 0:1], bank)
                    else:
                        p.rmax(m_run[:, 1:2], bank)
                        p.tt("vector", m_run[:, 0:1], m_run[:, 0:1], m_run[:, 1:2], ALU.max)
                    p.cp("vector", P_sb[:, jb * 512:(jb + 1) * 512], bank, inc=sV)
                p.ts("vector", m_run[:, 1:2], m_run[:, 0:1], -1.0, None, ALU.mult, inc=sV)
                p.wait("scalar", sV)
                p.act(P_sb, P_sb, AF.Exp, bias=m_run[:, 1:2], accum=l_run[:, 0:1], inc=sA)
                p.wait("tensor", sA)
                for jb32 in range(32):
                    tb = pb[2 + jb32 % 2]
                    p.wait("tensor", sV)
                    p.tp(tb[:, 0:128], P_sb[:, jb32 * 128:(jb32 + 1) * 128], cid, sP)
                    p.wait("vector", sP)
                    p.cp("vector", PT_sb[jb32 % 2], tb[:, 0:128], inc=sV)
                    p.wait("tensor", sV)
                    p.mm(V_sb[:, jb32 * 128:(jb32 + 1) * 128], PT_sb[jb32 % 2],
                         pb[6][:, 0:128], jb32 == 0, jb32 == 31,
                         sP if jb32 == 31 else None)
                kvfree = sP.n
                p.wait("vector", sP)
                p.recip(l_run[:, 1:2], l_run[:, 0:1], sV)
                p.wait("tensor", sV)
                p.tp(pb[7][0:1, 0:128], l_run[:, 1:2], cid, sP)
                p.wait("vector", sP)
                p.cp("vector", rows[0:1, 0:128], pb[7][0:1, 0:128], inc=sV)
                p.wait("tensor", sV)
                p.mm(c1r, rows[0:1, 0:128], pb[7][:, 0:128], True, True, sP)
                p.wait("vector", sP)
                ysl = y_sb[:, hd * TP + qt * 128: hd * TP + (qt + 1) * 128]
                p.cp("vector", ysl, pb[6][:, 0:128])
                p.tt("vector", ysl, ysl, pb[7][:, 0:128], ALU.mult, inc=sV)

        if lim < 6:
            return k
        for hc in range(HC):
            if wofree:
                p.wait("sync", sP, wofree)
            for c in range(HC):
                p.dma("sync", wosl[:, c * 128:(c + 1) * 128],
                      woT[c * 128:(c + 1) * 128, hc * 128:(hc + 1) * 128], dS)
            wo_ld = dS.n
            p.wait("tensor", dS, wo_ld)
            p.wait("tensor", sV)
            for c in range(HC):
                p.mm(wosl[:, c * 128:(c + 1) * 128], y_sb[:, c * TP:(c + 1) * TP],
                     pb[0 + hc % 2], c == 0, c == HC - 1, sP if c == HC - 1 else None)
            wofree = sP.n
            p.wait("vector", sP)
            skey = ("stage", hc % 2)
            if skey in stg_free:
                p.wait("vector", dG, stg_free[skey])
            p.cp("vector", stage[:, (hc % 2) * T:(hc % 2) * T + TP], pb[0 + hc % 2], inc=sV)
            p.wait("gpsimd", sV)
            p.wait("gpsimd", dG, regions.get(hc, 0))
            p.dma("gpsimd", accT[hc * 128:(hc + 1) * 128, t0:t0 + TP],
                  stage[:, (hc % 2) * T:(hc % 2) * T + TP], dG, accum=True)
            regions[hc] = dG.n
            stg_free[skey] = dG.n
    k.l1_attn_regions = dict(regions)
    return k


def build_l1_program(lim=99):
    nc = bass.Bass("TRN2", target_bir_lowering=False)
    k = Core(nc)
    k.l0 = {}
    build_l1(k, fused=False, lim=lim)
    finish(k)
    emit_block(nc, k.p)
    return nc


def l1_host_inputs(full, core, x1_slice):
    d = {}
    d["x1T"] = _cc(x1_slice)
    d["l1wqT"] = _cc(full["L1_attn_wq"].T)
    d["l1wkT"] = _cc(full["L1_attn_wk"].T)
    d["l1wvT"] = _cc(full["L1_attn_wv"].T)
    d["l1woT"] = _cc(full["L1_attn_wo"].T)
    d["W0T"] = _cc(full["L1_ffn_W0"].T)
    d["W1T"] = _cc(full["L1_ffn_W1"].T)
    d["b0"] = _cc(np.asarray(full["L1_ffn_bias0"]).reshape(N1 // 128, 128).T)
    d["b1"] = _cc(np.asarray(full["L1_ffn_bias1"]).reshape(HC, 128).T)
    pos = (core % 4) * T + np.arange(T)
    inv = 1.0 / (10000.0 ** (np.arange(0, HD, 2, dtype=np.float64) / HD))
    ang = pos[:, None] * inv[None, :]
    d["cosT"] = _cc(np.cos(ang).T.astype(np.float32))
    sn = np.sin(ang).T.astype(np.float64)
    sn[0:32] = -sn[0:32]
    d["sinT"] = _cc(sn)
    d["perm64"] = _cc(np.roll(np.eye(64), 32, axis=1))
    d["ident"] = np.eye(128, dtype=np.float32)
    d["ones1"] = np.ones((1, 128), dtype=np.float32)
    d["onescol"] = np.ones((128, 1), dtype=np.float32)
    return d


def kernel(**inputs):
    nc0 = build_l0_program()
    in0 = [l0_host_inputs(inputs, c) for c in range(8)]
    res0 = run_bass_kernel_spmd(nc0, in0, core_ids=list(range(8)))
    nc1 = build_l1_program()
    in1 = [l1_host_inputs(inputs, c, res0.results[c]["accT"]) for c in range(8)]
    res1 = run_bass_kernel_spmd(nc1, in1, core_ids=list(range(8)))
    out = np.zeros((2, 4096, H), np.float32)
    for c in range(8):
        b, s0 = c // 4, (c % 4) * T
        out[b, s0:s0 + T, :] = res1.results[c]["accT"].T
    return out
